# version 01
# baseline (speedup 1.0000x reference)
import math
from contextlib import ExitStack

import numpy as np
import concourse.bass as bass
import concourse.mybir as mybir
from concourse.bass_utils import run_bass_kernel_spmd

F32 = mybir.dt.float32
BF16 = mybir.dt.bfloat16
I32 = mybir.dt.int32
AF = mybir.ActivationFunctionType
ALU = mybir.AluOpType

NCORES = 8
B, SEQ, D = 4, 2048, 4096
OWN = 1024
H, HDIM = 32, 128
HID = 11008
DC = D // 128
TCN = SEQ // 128
KCN = SEQ // 128
HC = HID // 128
NS = 512
OSL = OWN // NS
INV_SQRT_HD = 1.0 / math.sqrt(HDIM)
LN_EPS = 1e-5

NF4_TABLE = np.array([
    -1.0, -0.6961928009986877, -0.5250730514526367, -0.39491748809814453,
    -0.28444138169288635, -0.18477343022823334, -0.09105003625154495, 0.0,
    0.07958029955625534, 0.16093020141124725, 0.24611230194568634,
    0.33791524171829224, 0.44070982933044434, 0.5626170039176941,
    0.6989939212799072, 1.0], dtype=np.float64)


def _nf4_poly_coeffs():
    s = np.arange(16, dtype=np.float64) - 7.5
    V = np.vander(s, 16, increasing=True)
    return np.linalg.solve(V, NF4_TABLE)


POLY = [float(v) for v in _nf4_poly_coeffs()]

ALL_STAGES = ("deq", "ln", "kv", "lnown", "attn", "wo", "f1", "f2")


class Ctr:
    def __init__(self, sem):
        self.sem = sem
        self.n = 0

    def inc(self, inst, k):
        inst.then_inc(self.sem, k)
        self.n += k
        return inst


class MK:
    def __init__(self, stages=ALL_STAGES, dumps=()):
        self.stages = stages
        self.dumps = set(dumps)
        nc = bass.Bass("TRN2", target_bir_lowering=False)
        self.nc = nc
        dd = lambda n: "ExternalOutput" if n in self.dumps else "Internal"

        self.x_t = nc.dram_tensor("x_t", [D, SEQ], F32, kind="ExternalInput")
        self.x_own_t = nc.dram_tensor("x_own_t", [D, OWN], F32, kind="ExternalInput")
        self.q0col = nc.dram_tensor("q0col", [128, 1], F32, kind="ExternalInput")
        self.wq_l = nc.dram_tensor("wq_l", [DC, 128, DC, 128], F32, kind="ExternalInput")
        self.wk_l = nc.dram_tensor("wk_l", [DC, 128, DC, 128], F32, kind="ExternalInput")
        self.wv_l = nc.dram_tensor("wv_l", [8, 128, DC, 512], F32, kind="ExternalInput")
        self.wo_l = nc.dram_tensor("wo_l", [DC, 128, DC, 128], F32, kind="ExternalInput")
        self.bq = nc.dram_tensor("bq", [1, D], F32, kind="ExternalInput")
        self.bk = nc.dram_tensor("bk", [1, D], F32, kind="ExternalInput")
        self.bv = nc.dram_tensor("bv", [1, D], F32, kind="ExternalInput")
        self.bo = nc.dram_tensor("bo", [1, D], F32, kind="ExternalInput")
        self.codes1 = nc.dram_tensor("codes1", [HID, D], I32, kind="ExternalInput")
        self.codes3 = nc.dram_tensor("codes3", [HID, D], I32, kind="ExternalInput")
        self.codes2 = nc.dram_tensor("codes2", [D, HID], I32, kind="ExternalInput")
        self.a1 = nc.dram_tensor("a1", [HID, 64], F32, kind="ExternalInput")
        self.a3 = nc.dram_tensor("a3", [HID, 64], F32, kind="ExternalInput")
        self.a2 = nc.dram_tensor("a2", [D, 172], F32, kind="ExternalInput")

        self.out_fm = nc.dram_tensor("out_fm", [D, OWN], F32, kind="ExternalOutput")

        self.w1raw = nc.dram_tensor("w1raw", [HID, D], BF16, kind=dd("w1raw"))
        self.w3raw = nc.dram_tensor("w3raw", [HID, D], BF16, kind=dd("w3raw"))
        self.w2raw = nc.dram_tensor("w2raw", [D, HID], BF16, kind=dd("w2raw"))
        self.w1t = nc.dram_tensor("w1t", [D, HID], BF16, kind=dd("w1t"))
        self.w3t = nc.dram_tensor("w3t", [D, HID], BF16, kind=dd("w3t"))
        self.w2t = nc.dram_tensor("w2t", [HID, D], BF16, kind=dd("w2t"))
        self.k_dram = nc.dram_tensor("k_dram", [D, SEQ], BF16, kind=dd("k_dram"))
        self.v_dram = nc.dram_tensor("v_dram", [H, TCN, 128, 128], BF16, kind=dd("v_dram"))
        self.h_dram = nc.dram_tensor("h_dram", [HID, OWN], BF16, kind=dd("h_dram"))
        self.resid_dram = nc.dram_tensor("resid_dram", [D, OWN], F32, kind=dd("resid_dram"))
        if "xn_full" in self.dumps:
            self.xnf_dump = nc.dram_tensor("xn_full_d", [D, SEQ], BF16, kind="ExternalOutput")
        if "xn_own" in self.dumps:
            self.xno_dump = nc.dram_tensor("xn_own_d", [D, OWN], BF16, kind="ExternalOutput")
        if "attn_all" in self.dumps:
            self.attn_dump = nc.dram_tensor("attn_all_d", [D, OWN], BF16, kind="ExternalOutput")

    def build(self):
        nc = self.nc
        with ExitStack() as ctx:
            self._sb = lambda name, shape, dt, c=ctx: c.enter_context(
                nc.sbuf_tensor(name, shape, dt))
            sem = lambda name: Ctr(ctx.enter_context(nc.semaphore(name)))
            self.s_ld = sem("s_ld")
            self.s_st = sem("s_st")
            self.s_cast = sem("s_cast")
            self.s_pe = sem("s_pe")
            self.s_dve = sem("s_dve")
            self.s_act = sem("s_act")
            self.s_pool = sem("s_pool")

            sb = self._sb
            self.ones_col_f = sb("ones_col_f", [128, 1], F32)
            self.ones_row_f = sb("ones_row_f", [1, 128], F32)
            self.ones_col_b = sb("ones_col_b", [128, 1], BF16)
            self.ones_row_b = sb("ones_row_b", [1, SEQ], BF16)
            self.ones_row_b128 = sb("ones_row_b128", [1, 128], BF16)
            self.iq_row = sb("iq_row", [128, OWN], F32)
            self.kcols = sb("kcols", [128, KCN], F32)
            self.pcol = sb("pcol", [128, 1], F32)
            self.q0_sb = sb("q0_sb", [128, 1], F32)

            self._setup()
            if "deq" in self.stages:
                self._dequant()
            if "ln" in self.stages:
                with ExitStack() as s_full:
                    self.xn_full = s_full.enter_context(
                        nc.sbuf_tensor("xn_full", [128, DC * SEQ], BF16))
                    self._layernorm(self.x_t, SEQ, self.xn_full,
                                    "xnf_dump" if "xn_full" in self.dumps else None,
                                    "f")
                    if "kv" in self.stages:
                        self._kv()
            if "lnown" in self.stages:
                with ExitStack() as s_own:
                    self.xn_own = s_own.enter_context(
                        nc.sbuf_tensor("xn_own", [128, DC * OWN], BF16))
                    self._layernorm(self.x_own_t, OWN, self.xn_own,
                                    "xno_dump" if "xn_own" in self.dumps else None,
                                    "o")
                    if "attn" in self.stages:
                        with ExitStack() as s_at:
                            self.attn_all = s_at.enter_context(
                                nc.sbuf_tensor("attn_all", [128, H * OWN], BF16))
                            self._attention()
                            if "wo" in self.stages:
                                self._wo()
                    if "f1" in self.stages:
                        self._f1()
            if "f2" in self.stages:
                self._f2_out()

            with nc.Block() as block:
                vals = (self.s_st.n, self.s_cast.n, self.s_ld.n, self.s_pe.n,
                        self.s_dve.n, self.s_act.n, self.s_pool.n)

                def fin(e, v=vals):
                    e.wait_ge(self.s_st.sem, v[0])
                    e.wait_ge(self.s_cast.sem, v[1])
                    e.wait_ge(self.s_ld.sem, v[2])
                    e.wait_ge(self.s_pe.sem, v[3])
                    e.wait_ge(self.s_dve.sem, v[4])
                    e.wait_ge(self.s_act.sem, v[5])
                    e.wait_ge(self.s_pool.sem, v[6])
                block.sync(fin)
        return nc

    def _barrier(self):
        vals = (self.s_ld.n, self.s_st.n, self.s_cast.n, self.s_pe.n,
                self.s_dve.n, self.s_act.n, self.s_pool.n)

        def waits(e, v=vals):
            e.wait_ge(self.s_ld.sem, v[0])
            e.wait_ge(self.s_st.sem, v[1])
            e.wait_ge(self.s_cast.sem, v[2])
            e.wait_ge(self.s_pe.sem, v[3])
            e.wait_ge(self.s_dve.sem, v[4])
            e.wait_ge(self.s_act.sem, v[5])
            e.wait_ge(self.s_pool.sem, v[6])

        with self.nc.Block() as block:
            block.sync(waits)
            block.gpsimd(waits)
            block.tensor(waits)
            block.vector(waits)
            block.scalar(waits)

    def _setup(self):
        nc = self.nc
        with nc.Block() as block:
            @block.sync
            def _(e):
                self.s_ld.inc(e.dma_start(out=self.q0_sb[:], in_=self.q0col[:]), 16)

            @block.gpsimd
            def _(g):
                g.iota(self.iq_row[:], [[1, OWN]], channel_multiplier=0,
                       allow_small_or_imprecise_dtypes=True)
                self.s_pool.inc(g.iota(self.pcol[:], [[0, 1]], channel_multiplier=1,
                                       allow_small_or_imprecise_dtypes=True), 1)

            pool_n, ld_n = self.s_pool.n, self.s_ld.n

            @block.vector
            def _(v):
                v.memset(self.ones_col_f[:], 1.0)
                v.memset(self.ones_row_f[:], 1.0)
                v.memset(self.ones_col_b[:], 1.0)
                v.memset(self.ones_row_b[:], 1.0)
                v.memset(self.ones_row_b128[:], 1.0)
                v.wait_ge(self.s_pool.sem, pool_n)
                v.wait_ge(self.s_ld.sem, ld_n)
                for kc in range(KCN):
                    v.scalar_tensor_tensor(self.kcols[:, kc:kc + 1], self.pcol[:],
                                           float(kc * 128), self.q0_sb[:],
                                           ALU.add, ALU.subtract)
                self.s_dve.inc(v.engine_nop(), 1)

    def _dequant(self):
        nc = self.nc
        blocks = []
        for codes, amax, nb, raw in ((self.codes1, self.a1, HC, self.w1raw),
                                     (self.codes3, self.a3, HC, self.w3raw)):
            for bidx in range(nb):
                blocks.append((codes, amax, bidx * 128, 0, D, raw))
        for bidx in range(DC):
            for col0, cols in ((0, 4096), (4096, 4096), (8192, 2816)):
                blocks.append((self.codes2, self.a2, bidx * 128, col0, cols, self.w2raw))

        with ExitStack() as c2:
            sb = lambda n, s, d: c2.enter_context(nc.sbuf_tensor(n, s, d))
            cb = [sb(f"dq_c{i}", [128, D], I32) for i in range(2)]
            ab = [sb(f"dq_a{i}", [128, 64], F32) for i in range(2)]
            s_t = sb("dq_s", [128, D], F32)
            t_t = sb("dq_t", [128, D], F32)
            ob = [sb(f"dq_o{i}", [128, D], BF16) for i in range(2)]

            ld0 = self.s_ld.n
            dve0 = self.s_dve.n
            st0 = self.s_st.n
            with nc.Block() as block:
                @block.sync
                def _(e):
                    for i, (codes, amax, r0, col0, cols, raw) in enumerate(blocks):
                        if i >= 2:
                            e.wait_ge(self.s_dve.sem, dve0 + (i - 1))
                        ablk = cols // 64
                        self.s_ld.inc(e.dma_start(out=cb[i % 2][:, 0:cols],
                                                  in_=codes[r0:r0 + 128, col0:col0 + cols]), 16)
                        self.s_ld.inc(e.dma_start(
                            out=ab[i % 2][:, 0:ablk],
                            in_=amax[r0:r0 + 128, col0 // 64:col0 // 64 + ablk]), 16)

                @block.vector
                def _(v):
                    for i, (codes, amax, r0, col0, cols, raw) in enumerate(blocks):
                        v.wait_ge(self.s_ld.sem, ld0 + 32 * (i + 1))
                        if i >= 2:
                            v.wait_ge(self.s_st.sem, st0 + 16 * (i - 1))
                        buf = i % 2
                        ablk = cols // 64
                        cs = cb[buf][:, 0:cols]
                        ss = s_t[:, 0:cols]
                        tt = t_t[:, 0:cols]
                        v.tensor_scalar(ss, cs, -7.5, None, ALU.add)
                        v.tensor_scalar(tt, ss, POLY[15], None, ALU.mult)
                        for k in range(14, 0, -1):
                            v.scalar_tensor_tensor(tt, tt, POLY[k], ss, ALU.add, ALU.mult)
                        aap = ab[buf].ap()
                        a_b = bass.AP(aap.tensor, aap.offset,
                                      [[aap.ap[0][0], 128], [1, ablk], [0, 64]])
                        self.s_dve.inc(
                            v.scalar_tensor_tensor(ob[buf][:, 0:cols], tt, POLY[0],
                                                   a_b, ALU.add, ALU.mult), 1)

                @block.scalar
                def _(e):
                    for i, (codes, amax, r0, col0, cols, raw) in enumerate(blocks):
                        e.wait_ge(self.s_dve.sem, dve0 + (i + 1))
                        self.s_st.inc(e.dma_start(out=raw[r0:r0 + 128, col0:col0 + cols],
                                                  in_=ob[i % 2][:, 0:cols]), 16)

        with ExitStack() as c2:
            tp = [c2.enter_context(nc.sbuf_tensor(f"tp{i}", [128, HID], BF16))
                  for i in range(2)]
            tspecs = [(self.w1raw, self.w1t, DC, HID), (self.w3raw, self.w3t, DC, HID),
                      (self.w2raw, self.w2t, HC, D)]
            st1 = self.s_st.n
            ld1 = self.s_ld.n
            with nc.Block() as block:
                @block.sync
                def _(e):
                    e.wait_ge(self.s_st.sem, st1)
                    i = 0
                    for (raw, dstt, nchunk, width) in tspecs:
                        for ccn in range(nchunk):
                            if i >= 2:
                                e.wait_ge(self.s_st.sem, st1 + 16 * (i - 1))
                            self.s_ld.inc(
                                e.dma_start(out=tp[i % 2][:, 0:width],
                                            in_=raw[:, ccn * 128:(ccn + 1) * 128],
                                            transpose=True), 16)
                            i += 1

                @block.scalar
                def _(e):
                    i = 0
                    for (raw, dstt, nchunk, width) in tspecs:
                        for ccn in range(nchunk):
                            e.wait_ge(self.s_ld.sem, ld1 + 16 * (i + 1))
                            self.s_st.inc(
                                e.dma_start(out=dstt[ccn * 128:(ccn + 1) * 128, :],
                                            in_=tp[i % 2][:, 0:width]), 16)
                            i += 1

    def _layernorm(self, x_src, ntok, xn_dst, dump_attr, tag):
        nc = self.nc
        nsl = ntok // NS
        self._barrier()
        with ExitStack() as c2:
            sb = lambda n, s, d: c2.enter_context(nc.sbuf_tensor(n, s, d))
            xt = [sb(f"ln{tag}_x{i}", [128, ntok], F32) for i in range(2)]
            sq = sb(f"ln{tag}_sq", [128, ntok], F32)
            mu_r = sb(f"ln{tag}_mu", [1, ntok], F32)[:]
            var_r = sb(f"ln{tag}_var", [1, ntok], F32)[:]
            rst_r = sb(f"ln{tag}_rst", [1, ntok], F32)[:]
            tmp_r = sq[0:1, :]
            std_r = sq[0:1, :]

            ld0, act0, pe0, dve0 = self.s_ld.n, self.s_act.n, self.s_pe.n, self.s_dve.n
            with ExitStack() as c3:
                ps_sx = c3.enter_context(nc.psum_tensor(f"ps{tag}_sx", [1, ntok], F32))
                ps_sq = c3.enter_context(nc.psum_tensor(f"ps{tag}_sq", [1, ntok], F32))
                with nc.Block() as block:
                    @block.sync
                    def _(e):
                        for dc in range(DC):
                            if dc >= 2:
                                e.wait_ge(self.s_pe.sem, pe0 + dc - 1)
                                e.wait_ge(self.s_act.sem, act0 + dc - 1)
                            self.s_ld.inc(
                                e.dma_start(out=xt[dc % 2][:],
                                            in_=x_src[dc * 128:(dc + 1) * 128, :]), 16)

                    @block.scalar
                    def _(e):
                        for dc in range(DC):
                            e.wait_ge(self.s_ld.sem, ld0 + 16 * (dc + 1))
                            if dc >= 1:
                                e.wait_ge(self.s_pe.sem, pe0 + dc)
                            self.s_act.inc(
                                e.activation(sq[:], xt[dc % 2][:], AF.Square), 1)

                    @block.tensor
                    def _(t):
                        for dc in range(DC):
                            t.wait_ge(self.s_act.sem, act0 + dc + 1)
                            for sl in range(nsl):
                                s = slice(sl * NS, (sl + 1) * NS)
                                t.matmul(ps_sx[:, s], self.ones_col_f[:],
                                         xt[dc % 2][:, s],
                                         start=(dc == 0), stop=(dc == DC - 1))
                                mi = t.matmul(ps_sq[:, s], self.ones_col_f[:],
                                              sq[:, s],
                                              start=(dc == 0), stop=(dc == DC - 1))
                            self.s_pe.inc(mi, 1)

                    pe1 = self.s_pe.n

                    @block.vector
                    def _(v):
                        v.wait_ge(self.s_pe.sem, pe1)
                        v.tensor_scalar(mu_r, ps_sx[:], 1.0 / D, None, ALU.mult)
                        v.tensor_scalar(var_r, ps_sq[:], 1.0 / D, LN_EPS, ALU.mult,
                                        ALU.add)
                        v.tensor_mul(tmp_r, mu_r, mu_r)
                        self.s_dve.inc(v.tensor_sub(var_r, var_r, tmp_r), 1)

                    dve1 = self.s_dve.n

                    @block.scalar
                    def _(e):
                        e.wait_ge(self.s_dve.sem, dve1)
                        self.s_act.inc(e.activation(std_r, var_r, AF.Sqrt), 1)

                    act2 = self.s_act.n

                    @block.vector
                    def _(v):
                        v.wait_ge(self.s_act.sem, act2)
                        self.s_dve.inc(v.reciprocal(rst_r, std_r), 1)

                dve2 = self.s_dve.n

            with ExitStack() as c3:
                ps_mu = c3.enter_context(nc.psum_tensor(f"ps{tag}_mu", [128, ntok], F32))
                ps_rs = c3.enter_context(nc.psum_tensor(f"ps{tag}_rs", [128, ntok], F32))
                pe2 = self.s_pe.n + 1
                with nc.Block() as block:
                    @block.tensor
                    def _(t):
                        t.wait_ge(self.s_dve.sem, dve2)
                        for sl in range(nsl):
                            s = slice(sl * NS, (sl + 1) * NS)
                            t.matmul(ps_mu[:, s], self.ones_row_f[:], mu_r[:, s],
                                     start=True, stop=True)
                            mi = t.matmul(ps_rs[:, s], self.ones_row_f[:],
                                          rst_r[:, s], start=True, stop=True)
                        self.s_pe.inc(mi, 1)

                ld1 = self.s_ld.n
                with nc.Block() as block:
                    @block.sync
                    def _(e):
                        e.wait_ge(self.s_pe.sem, pe2)
                        for dc in range(DC):
                            if dc >= 2:
                                e.wait_ge(self.s_dve.sem, dve2 + dc - 1)
                            self.s_ld.inc(
                                e.dma_start(out=xt[dc % 2][:],
                                            in_=x_src[dc * 128:(dc + 1) * 128, :]), 16)

                    @block.vector
                    def _(v):
                        v.wait_ge(self.s_pe.sem, pe2)
                        for dc in range(DC):
                            v.wait_ge(self.s_ld.sem, ld1 + 16 * (dc + 1))
                            xs = xt[dc % 2][:]
                            v.tensor_sub(sq[:], xs, ps_mu[:])
                            self.s_dve.inc(
                                v.tensor_mul(xn_dst[:, dc * ntok:(dc + 1) * ntok],
                                             sq[:], ps_rs[:]), 1)

                if dump_attr is not None:
                    dvex = self.s_dve.n
                    dmp = getattr(self, dump_attr)
                    with nc.Block() as block2:
                        @block2.scalar
                        def _(e):
                            e.wait_ge(self.s_dve.sem, dvex)
                            for dc in range(DC):
                                self.s_st.inc(
                                    e.dma_start(out=dmp[dc * 128:(dc + 1) * 128, :],
                                                in_=xn_dst[:, dc * ntok:(dc + 1) * ntok]), 16)

    def _kv(self):
        nc = self.nc
        self._barrier()
        with ExitStack() as c2:
            sb = lambda n, s, d: c2.enter_context(nc.sbuf_tensor(n, s, d))
            wkc = [sb(f"kv_wk{i}", [128, D], BF16) for i in range(2)]
            kf = [sb(f"kv_kf{i}", [128, SEQ], BF16) for i in range(2)]
            bk_b = sb("bk_b", [1, D], BF16)
            with ExitStack() as c3:
                ps_k = [c3.enter_context(nc.psum_tensor(f"ps_k{i}", [128, SEQ], F32))
                        for i in range(2)]
                cast0, pe0, act0, st0 = (self.s_cast.n, self.s_pe.n,
                                         self.s_act.n, self.s_st.n)
                with nc.Block() as block:
                    @block.gpsimd
                    def _(g):
                        self.s_cast.inc(g.dma_start(out=bk_b[:], in_=self.bk[:]), 16)
                        g.wait_ge(self.s_cast.sem, self.s_cast.n)
                        for fc in range(DC):
                            if fc >= 2:
                                g.wait_ge(self.s_pe.sem, pe0 + fc - 1)
                            self.s_cast.inc(g.dma_start(
                                out=wkc[fc % 2][:],
                                in_=self.wk_l[fc].rearrange("p dc j -> p (dc j)")), 16)
                            g.wait_ge(self.s_cast.sem, self.s_cast.n)

                    @block.tensor
                    def _(t):
                        for fc in range(DC):
                            t.wait_ge(self.s_cast.sem, cast0 + 16 * (fc + 2))
                            if fc >= 2:
                                t.wait_ge(self.s_act.sem, act0 + fc - 1)
                            pk = ps_k[fc % 2]
                            for dc in range(DC):
                                for sl in range(SEQ // NS):
                                    c0 = sl * NS
                                    t.matmul(pk[:, c0:c0 + NS],
                                             wkc[fc % 2][:, dc * 128:(dc + 1) * 128],
                                             self.xn_full[:, dc * SEQ + c0:dc * SEQ + c0 + NS],
                                             start=(dc == 0), stop=False)
                            for sl in range(SEQ // NS):
                                c0 = sl * NS
                                mi = t.matmul(pk[:, c0:c0 + NS],
                                              bk_b[:, fc * 128:(fc + 1) * 128],
                                              self.ones_row_b[:, c0:c0 + NS],
                                              start=False, stop=True)
                            self.s_pe.inc(mi, 1)

                    @block.scalar
                    def _(e):
                        for fc in range(DC):
                            e.wait_ge(self.s_pe.sem, pe0 + fc + 1)
                            if fc >= 2:
                                e.wait_ge(self.s_st.sem, st0 + 16 * (fc - 1))
                            self.s_act.inc(e.copy(kf[fc % 2][:], ps_k[fc % 2][:]), 1)
                            e.wait_ge(self.s_act.sem, self.s_act.n)
                            self.s_st.inc(e.dma_start(
                                out=self.k_dram[fc * 128:(fc + 1) * 128, :],
                                in_=kf[fc % 2][:]), 16)

        with ExitStack() as c2:
            sb = lambda n, s, d: c2.enter_context(nc.sbuf_tensor(n, s, d))
            wvb = sb("kv_wv", [128, DC * 512], BF16)
            vf = [sb(f"kv_vf{i}", [128, 512], BF16) for i in range(2)]
            bv_b = sb("bv_b", [1, D], BF16)
            with ExitStack() as c3:
                ps_v = [c3.enter_context(nc.psum_tensor(f"ps_v{i}", [128, 512], F32))
                        for i in range(2)]
                cast1, pe1, act1, st1 = (self.s_cast.n, self.s_pe.n,
                                         self.s_act.n, self.s_st.n)
                with nc.Block() as block:
                    @block.gpsimd
                    def _(g):
                        self.s_cast.inc(g.dma_start(out=bv_b[:], in_=self.bv[:]), 16)
                        g.wait_ge(self.s_cast.sem, self.s_cast.n)
                        for fg in range(8):
                            if fg >= 1:
                                g.wait_ge(self.s_pe.sem, pe1 + fg * TCN)
                            self.s_cast.inc(g.dma_start(
                                out=wvb[:],
                                in_=self.wv_l[fg].rearrange("p dc j -> p (dc j)")), 16)
                            g.wait_ge(self.s_cast.sem, self.s_cast.n)

                    @block.tensor
                    def _(t):
                        for fg in range(8):
                            t.wait_ge(self.s_cast.sem, cast1 + 16 * (fg + 2))
                            for tc in range(TCN):
                                it = fg * TCN + tc
                                if it >= 2:
                                    t.wait_ge(self.s_act.sem, act1 + it - 1)
                                pv = ps_v[it % 2]
                                for dc in range(DC):
                                    t.matmul(pv[:],
                                             self.xn_full[:, dc * SEQ + tc * 128:
                                                          dc * SEQ + (tc + 1) * 128],
                                             wvb[:, dc * 512:(dc + 1) * 512],
                                             start=(dc == 0), stop=False)
                                mi = t.matmul(pv[:], self.ones_row_b128[:],
                                              bv_b[:, fg * 512:(fg + 1) * 512],
                                              start=False, stop=True)
                                self.s_pe.inc(mi, 1)

                    @block.scalar
                    def _(e):
                        for fg in range(8):
                            for tc in range(TCN):
                                it = fg * TCN + tc
                                e.wait_ge(self.s_pe.sem, pe1 + it + 1)
                                if it >= 2:
                                    e.wait_ge(self.s_st.sem, st1 + 16 * (it - 1))
                                self.s_act.inc(e.copy(vf[it % 2][:], ps_v[it % 2][:]), 1)
                                e.wait_ge(self.s_act.sem, self.s_act.n)
                                dst = self.v_dram[fg * 4:(fg + 1) * 4, tc].rearrange(
                                    "h p j -> p h j")
                                src = vf[it % 2].rearrange("p (h j) -> p h j", h=4)
                                self.s_st.inc(e.dma_start(out=dst, in_=src), 16)

    def _attention(self):
        nc = self.nc
        self._barrier()
        with ExitStack() as c2:
            sb = lambda n, s, d: c2.enter_context(nc.sbuf_tensor(n, s, d))
            wqc_t = sb("at_wq", [128, D], BF16)
            wqc = [wqc_t, wqc_t]
            kh = [sb(f"at_kh{i}", [128, SEQ], BF16) for i in range(2)]
            vh = [sb(f"at_vh{i}", [128, SEQ], BF16) for i in range(2)]
            q_sb = sb("at_q", [128, OWN], BF16)
            exp_sb = [sb(f"at_e{i}", [128, OWN], F32) for i in range(2)]
            pt = [sb(f"at_p{i}", [128, OWN], BF16) for i in range(2)]
            sums = sb("at_sums", [1, OWN], F32)[:]
            recip = sb("at_recip", [1, OWN], F32)[:]
            rb = sb("at_rb", [128, OWN], F32)
            bq_b = sb("bq_b", [1, D], BF16)

            with self.nc.Block() as block0:
                @block0.gpsimd
                def _(g):
                    self.s_cast.inc(g.dma_start(out=bq_b[:], in_=self.bq[:]), 16)
                    g.wait_ge(self.s_cast.sem, self.s_cast.n)

            prev_pe_done = {}
            prev_qe_done = {}
            prev_q_pe = {}

            for h in range(H):
                with ExitStack() as c3, nc.Block() as block:
                    ps_s = [c3.enter_context(
                        nc.psum_tensor(f"ps_s{h}_{i}", [128, OWN], F32)) for i in range(2)]
                    ps_av = c3.enter_context(nc.psum_tensor(f"ps_av{h}", [128, OWN], F32))
                    ps_sum = c3.enter_context(nc.psum_tensor(f"ps_sum{h}", [1, OWN], F32))
                    hb = h % 2

                    ld0, cast0 = self.s_ld.n, self.s_cast.n
                    pe0, dve0, act0 = self.s_pe.n, self.s_dve.n, self.s_act.n
                    q_evac_pos = act0 + 1
                    exp_pos = lambda kc: act0 + 2 + kc
                    mask_pos = lambda kc: dve0 + 1 + kc
                    recip_pos = dve0 + KCN + 1
                    evac_pos = dve0 + KCN + 2

                    @block.sync
                    def _(e):
                        if h >= 2:
                            e.wait_ge(self.s_pe.sem, prev_pe_done[h - 2])
                        self.s_ld.inc(e.dma_start(
                            out=kh[hb][:], in_=self.k_dram[h * 128:(h + 1) * 128, :]), 16)
                        self.s_ld.inc(e.dma_start(
                            out=vh[hb].rearrange("p (kc j) -> p kc j", kc=KCN),
                            in_=self.v_dram[h].rearrange("kc p j -> p kc j")), 16)

                    @block.gpsimd
                    def _(g):
                        if h >= 1:
                            g.wait_ge(self.s_pe.sem, prev_q_pe[h - 1])
                        self.s_cast.inc(g.dma_start(
                            out=wqc[hb][:],
                            in_=self.wq_l[h].rearrange("p dc j -> p (dc j)")), 16)
                        g.wait_ge(self.s_cast.sem, self.s_cast.n)

                    pe_pos = {}

                    @block.tensor
                    def _(t):
                        cnt = pe0
                        t.wait_ge(self.s_cast.sem, cast0 + 16)
                        if h >= 1:
                            t.wait_ge(self.s_act.sem, prev_qe_done[h - 1])
                        for dc in range(DC):
                            for sl in range(OSL):
                                c0 = sl * NS
                                t.matmul(ps_s[0][:, c0:c0 + NS],
                                         wqc[hb][:, dc * 128:(dc + 1) * 128],
                                         self.xn_own[:, dc * OWN + c0:dc * OWN + c0 + NS],
                                         start=(dc == 0), stop=False)
                        for sl in range(OSL):
                            c0 = sl * NS
                            mi = t.matmul(ps_s[0][:, c0:c0 + NS],
                                          bq_b[:, h * 128:(h + 1) * 128],
                                          self.ones_row_b[:, c0:c0 + NS],
                                          start=False, stop=True)
                        cnt += 1
                        self.s_pe.inc(mi, 1)
                        pe_pos["q"] = cnt
                        prev_q_pe[h] = cnt

                        t.wait_ge(self.s_act.sem, q_evac_pos)
                        t.wait_ge(self.s_ld.sem, ld0 + 32)
                        for kc in range(KCN):
                            sbuf = kc % 2
                            if kc >= 2:
                                t.wait_ge(self.s_act.sem, exp_pos(kc - 2))
                            for sl in range(OSL):
                                c0 = sl * NS
                                mi = t.matmul(ps_s[sbuf][:, c0:c0 + NS],
                                              kh[hb][:, kc * 128:(kc + 1) * 128],
                                              q_sb[:, c0:c0 + NS],
                                              start=True, stop=True)
                            cnt += 1
                            self.s_pe.inc(mi, 1)
                            pe_pos[("s", kc)] = cnt
                            if kc >= 1:
                                j = kc - 1
                                t.wait_ge(self.s_dve.sem, mask_pos(j))
                                for sl in range(OSL):
                                    c0 = sl * NS
                                    t.matmul(ps_sum[:, c0:c0 + NS], self.ones_col_b[:],
                                             pt[j % 2][:, c0:c0 + NS],
                                             start=(j == 0), stop=(j == KCN - 1))
                                    mi = t.matmul(ps_av[:, c0:c0 + NS],
                                                  vh[hb][:, j * 128:(j + 1) * 128],
                                                  pt[j % 2][:, c0:c0 + NS],
                                                  start=(j == 0),
                                                  stop=(j == KCN - 1))
                                cnt += 1
                                self.s_pe.inc(mi, 1)
                                pe_pos[("av", j)] = cnt
                        j = KCN - 1
                        t.wait_ge(self.s_dve.sem, mask_pos(j))
                        for sl in range(OSL):
                            c0 = sl * NS
                            t.matmul(ps_sum[:, c0:c0 + NS], self.ones_col_b[:],
                                     pt[j % 2][:, c0:c0 + NS], start=False, stop=True)
                            mi = t.matmul(ps_av[:, c0:c0 + NS],
                                          vh[hb][:, j * 128:(j + 1) * 128],
                                          pt[j % 2][:, c0:c0 + NS],
                                          start=False, stop=True)
                        cnt += 1
                        self.s_pe.inc(mi, 1)
                        pe_pos[("av", j)] = cnt
                        t.wait_ge(self.s_dve.sem, recip_pos)
                        for sl in range(OSL):
                            c0 = sl * NS
                            mi = t.matmul(ps_s[0][:, c0:c0 + NS],
                                          self.ones_row_f[:],
                                          recip[:, c0:c0 + NS],
                                          start=True, stop=True)
                        cnt += 1
                        self.s_pe.inc(mi, 1)
                        pe_pos["rb"] = cnt
                        prev_pe_done[h] = cnt

                    @block.scalar
                    def _(e):
                        e.wait_ge(self.s_pe.sem, pe_pos["q"])
                        self.s_act.inc(e.copy(q_sb[:], ps_s[0][:]), 1)
                        prev_qe_done[h] = self.s_act.n
                        for kc in range(KCN):
                            e.wait_ge(self.s_pe.sem, pe_pos[("s", kc)])
                            if kc >= 2:
                                e.wait_ge(self.s_dve.sem, mask_pos(kc - 2))
                            self.s_act.inc(e.activation(exp_sb[kc % 2][:],
                                                        ps_s[kc % 2][:], AF.Exp,
                                                        scale=INV_SQRT_HD), 1)

                    @block.vector
                    def _(v):
                        for kc in range(KCN):
                            v.wait_ge(self.s_act.sem, exp_pos(kc))
                            if kc >= 2:
                                v.wait_ge(self.s_pe.sem, pe_pos[("av", kc - 2)])
                            self.s_dve.inc(v.scalar_tensor_tensor(
                                pt[kc % 2][:], self.iq_row[:],
                                self.kcols[:, kc:kc + 1], exp_sb[kc % 2][:],
                                ALU.is_ge, ALU.mult), 1)
                        v.wait_ge(self.s_pe.sem, pe_pos[("av", KCN - 1)])
                        v.tensor_copy(sums[:], ps_sum[:])
                        self.s_dve.inc(v.reciprocal(recip[:], sums[:]), 1)
                        v.wait_ge(self.s_pe.sem, pe_pos["rb"])
                        v.tensor_copy(rb[:], ps_s[0][:])
                        self.s_dve.inc(v.tensor_mul(
                            self.attn_all[:, h * OWN:(h + 1) * OWN], ps_av[:], rb[:]), 1)

            if "attn_all" in self.dumps:
                dven = self.s_dve.n
                with nc.Block() as block:
                    @block.scalar
                    def _(e):
                        e.wait_ge(self.s_dve.sem, dven)
                        for h in range(H):
                            self.s_st.inc(e.dma_start(
                                out=self.attn_dump[h * 128:(h + 1) * 128, :],
                                in_=self.attn_all[:, h * OWN:(h + 1) * OWN]), 16)

    def _wo(self):
        nc = self.nc
        self._barrier()
        with ExitStack() as c2:
            sb = lambda n, s, d: c2.enter_context(nc.sbuf_tensor(n, s, d))
            woc = [sb(f"wo_w{i}", [128, D], BF16) for i in range(2)]
            xoc = [sb(f"wo_x{i}", [128, OWN], F32) for i in range(2)]
            rs = [sb(f"wo_r{i}", [128, OWN], F32) for i in range(2)]
            bo_b = sb("bo_b", [1, D], BF16)
            with ExitStack() as c3:
                ps_o = [c3.enter_context(nc.psum_tensor(f"ps_o{i}", [128, OWN], F32))
                        for i in range(2)]
                cast0, ld0 = self.s_cast.n, self.s_ld.n
                pe0, dve0, st0 = self.s_pe.n, self.s_dve.n, self.s_st.n
                with nc.Block() as block:
                    @block.gpsimd
                    def _(g):
                        self.s_cast.inc(g.dma_start(out=bo_b[:], in_=self.bo[:]), 16)
                        g.wait_ge(self.s_cast.sem, self.s_cast.n)
                        for oc in range(DC):
                            if oc >= 2:
                                g.wait_ge(self.s_pe.sem, pe0 + oc - 1)
                            self.s_cast.inc(g.dma_start(
                                out=woc[oc % 2][:],
                                in_=self.wo_l[oc].rearrange("p dc j -> p (dc j)")), 16)
                            g.wait_ge(self.s_cast.sem, self.s_cast.n)

                    @block.sync
                    def _(e):
                        for oc in range(DC):
                            if oc >= 2:
                                e.wait_ge(self.s_dve.sem, dve0 + oc - 1)
                            self.s_ld.inc(e.dma_start(
                                out=xoc[oc % 2][:],
                                in_=self.x_own_t[oc * 128:(oc + 1) * 128, :]), 16)

                    @block.tensor
                    def _(t):
                        for oc in range(DC):
                            t.wait_ge(self.s_cast.sem, cast0 + 16 * (oc + 2))
                            if oc >= 2:
                                t.wait_ge(self.s_dve.sem, dve0 + oc - 1)
                            po = ps_o[oc % 2]
                            for dcp in range(DC):
                                for sl in range(OSL):
                                    c0 = sl * NS
                                    t.matmul(po[:, c0:c0 + NS],
                                             woc[oc % 2][:, dcp * 128:(dcp + 1) * 128],
                                             self.attn_all[:, dcp * OWN + c0:
                                                           dcp * OWN + c0 + NS],
                                             start=(dcp == 0), stop=False)
                            for sl in range(OSL):
                                c0 = sl * NS
                                mi = t.matmul(po[:, c0:c0 + NS],
                                              bo_b[:, oc * 128:(oc + 1) * 128],
                                              self.ones_row_b[:, c0:c0 + NS],
                                              start=False, stop=True)
                            self.s_pe.inc(mi, 1)

                    @block.vector
                    def _(v):
                        for oc in range(DC):
                            v.wait_ge(self.s_pe.sem, pe0 + oc + 1)
                            v.wait_ge(self.s_ld.sem, ld0 + 16 * (oc + 1))
                            if oc >= 2:
                                v.wait_ge(self.s_st.sem, st0 + 16 * (oc - 1))
                            self.s_dve.inc(v.tensor_add(rs[oc % 2][:], ps_o[oc % 2][:],
                                                        xoc[oc % 2][:]), 1)

                    @block.scalar
                    def _(e):
                        for oc in range(DC):
                            e.wait_ge(self.s_dve.sem, dve0 + oc + 1)
                            self.s_st.inc(e.dma_start(
                                out=self.resid_dram[oc * 128:(oc + 1) * 128, :],
                                in_=rs[oc % 2][:]), 16)

    def _f1(self):
        nc = self.nc
        self._barrier()
        groups = []
        h0 = 0
        while h0 < HC:
            g = min(4, HC - h0)
            groups.append((h0, g))
            h0 += g
        dve_before = [0]
        for (_, g) in groups:
            dve_before.append(dve_before[-1] + 2 * g)
        act_before = [0]
        for (_, g) in groups:
            act_before.append(act_before[-1] + g)

        with ExitStack() as c2:
            sb = lambda n, s, d: c2.enter_context(nc.sbuf_tensor(n, s, d))
            w1b = sb("f1_w1", [128, DC * 512], BF16)
            w3b = sb("f1_w3", [128, DC * 512], BF16)
            a_sb = sb("f1_a", [128, 4 * OWN], F32)
            sl_sb = sb("f1_sl", [128, OWN], F32)
            h_bf = [sb(f"f1_h{i}", [128, OWN], BF16) for i in range(2)]
            with ExitStack() as c3:
                ps = [c3.enter_context(nc.psum_tensor(f"ps_f1_{i}", [128, OWN], F32))
                      for i in range(4)]
                ld0, pe0, dve0, act0, st0 = (self.s_ld.n, self.s_pe.n, self.s_dve.n,
                                             self.s_act.n, self.s_st.n)
                with nc.Block() as block:
                    @block.sync
                    def _(e):
                        for gi, (hh0, g) in enumerate(groups):
                            if gi >= 1:
                                e.wait_ge(self.s_pe.sem, pe0 + 2 * gi)
                            w = g * 128
                            for (dst, srcT) in ((w1b, self.w1t), (w3b, self.w3t)):
                                src = srcT[:, hh0 * 128:hh0 * 128 + w]
                                self.s_ld.inc(e.dma_start(
                                    out=dst.rearrange("p (dc j) -> p dc j",
                                                      dc=DC)[:, :, 0:w],
                                    in_=src.rearrange("(dc p) j -> p dc j", p=128)), 16)

                    @block.tensor
                    def _(t):
                        for gi, (hh0, g) in enumerate(groups):
                            t.wait_ge(self.s_ld.sem, ld0 + 32 * (gi + 1))
                            if gi >= 1:
                                t.wait_ge(self.s_dve.sem, dve0 + dve_before[gi])
                            for sub in range(g):
                                for dc in range(DC):
                                    for sl in range(OSL):
                                        c0 = sl * NS
                                        mi = t.matmul(ps[sub][:, c0:c0 + NS],
                                                      w1b[:, dc * 512 + sub * 128:
                                                          dc * 512 + (sub + 1) * 128],
                                                      self.xn_own[:, dc * OWN + c0:
                                                                  dc * OWN + c0 + NS],
                                                      start=(dc == 0),
                                                      stop=(dc == DC - 1))
                            self.s_pe.inc(mi, 1)
                            t.wait_ge(self.s_dve.sem, dve0 + dve_before[gi] + g)
                            for sub in range(g):
                                for dc in range(DC):
                                    for sl in range(OSL):
                                        c0 = sl * NS
                                        mi = t.matmul(ps[sub][:, c0:c0 + NS],
                                                      w3b[:, dc * 512 + sub * 128:
                                                          dc * 512 + (sub + 1) * 128],
                                                      self.xn_own[:, dc * OWN + c0:
                                                                  dc * OWN + c0 + NS],
                                                      start=(dc == 0),
                                                      stop=(dc == DC - 1))
                            self.s_pe.inc(mi, 1)

                    @block.vector
                    def _(v):
                        for gi, (hh0, g) in enumerate(groups):
                            v.wait_ge(self.s_pe.sem, pe0 + 2 * gi + 1)
                            for sub in range(g):
                                self.s_dve.inc(v.tensor_copy(
                                    a_sb[:, sub * OWN:(sub + 1) * OWN], ps[sub][:]), 1)
                            v.wait_ge(self.s_pe.sem, pe0 + 2 * gi + 2)
                            for sub in range(g):
                                v.wait_ge(self.s_act.sem,
                                          act0 + act_before[gi] + sub + 1)
                                hcx = hh0 + sub
                                if hcx >= 2:
                                    v.wait_ge(self.s_st.sem, st0 + 16 * (hcx - 1))
                                self.s_dve.inc(v.tensor_mul(
                                    h_bf[hcx % 2][:], sl_sb[:], ps[sub][:]), 1)

                    @block.scalar
                    def _(e):
                        for gi, (hh0, g) in enumerate(groups):
                            for sub in range(g):
                                e.wait_ge(self.s_dve.sem,
                                          dve0 + dve_before[gi] + sub + 1)
                                if gi > 0 or sub > 0:
                                    e.wait_ge(self.s_dve.sem,
                                              dve0 + dve_before[gi] + g + sub)
                                self.s_act.inc(e.activation(
                                    sl_sb[:], a_sb[:, sub * OWN:(sub + 1) * OWN],
                                    AF.Silu), 1)
                                e.wait_ge(self.s_dve.sem,
                                          dve0 + dve_before[gi] + g + sub + 1)
                                hcx = hh0 + sub
                                self.s_st.inc(e.dma_start(
                                    out=self.h_dram[hcx * 128:(hcx + 1) * 128, :],
                                    in_=h_bf[hcx % 2][:]), 16)

    def _f2_out(self):
        nc = self.nc
        self._barrier()
        groups = []
        h0 = 0
        while h0 < HC:
            g = min(4, HC - h0)
            groups.append((h0, g))
            h0 += g
        it_before = [0]
        for (_, g) in groups:
            it_before.append(it_before[-1] + DC)

        with ExitStack() as c2:
            sb = lambda n, s, d: c2.enter_context(nc.sbuf_tensor(n, s, d))
            acc = sb("f2_acc", [128, DC * OWN], F32)
            w2b = sb("f2_w2", [128, 4 * D], BF16)
            hbl = sb("f2_h", [128, 4 * OWN], BF16)
            with ExitStack() as c3:
                ps = [c3.enter_context(nc.psum_tensor(f"ps_f2_{i}", [128, OWN], F32))
                      for i in range(2)]
                ld0, pe0, dve0 = self.s_ld.n, self.s_pe.n, self.s_dve.n
                nld = [0]
                acc_loads = DC
                tot = acc_loads
                for (_, g) in groups:
                    tot += 2 * g
                    nld.append(tot)
                with nc.Block() as block:
                    @block.sync
                    def _(e):
                        for oc in range(DC):
                            self.s_ld.inc(e.dma_start(
                                out=acc[:, oc * OWN:(oc + 1) * OWN],
                                in_=self.resid_dram[oc * 128:(oc + 1) * 128, :]), 16)
                        for gi, (hh0, g) in enumerate(groups):
                            if gi >= 1:
                                e.wait_ge(self.s_pe.sem, pe0 + it_before[gi])
                            for sub in range(g):
                                hcx = hh0 + sub
                                self.s_ld.inc(e.dma_start(
                                    out=w2b[:, sub * D:(sub + 1) * D],
                                    in_=self.w2t[hcx * 128:(hcx + 1) * 128, :]), 16)
                                self.s_ld.inc(e.dma_start(
                                    out=hbl[:, sub * OWN:(sub + 1) * OWN],
                                    in_=self.h_dram[hcx * 128:(hcx + 1) * 128, :]), 16)

                    @block.tensor
                    def _(t):
                        for gi, (hh0, g) in enumerate(groups):
                            t.wait_ge(self.s_ld.sem,
                                      ld0 + 16 * (acc_loads + nld[gi + 1] - acc_loads))
                            for dc in range(DC):
                                it = it_before[gi] + dc
                                if it >= 2:
                                    t.wait_ge(self.s_dve.sem, dve0 + it - 1)
                                p = ps[it % 2]
                                for sl in range(OSL):
                                    c0 = sl * NS
                                    for sub in range(g):
                                        mi = t.matmul(p[:, c0:c0 + NS],
                                                      w2b[:, sub * D + dc * 128:
                                                          sub * D + (dc + 1) * 128],
                                                      hbl[:, sub * OWN + c0:
                                                          sub * OWN + c0 + NS],
                                                      start=(sub == 0),
                                                      stop=(sub == g - 1))
                                self.s_pe.inc(mi, 1)

                    @block.vector
                    def _(v):
                        v.wait_ge(self.s_ld.sem, ld0 + 16 * acc_loads)
                        for gi, (hh0, g) in enumerate(groups):
                            for dc in range(DC):
                                it = it_before[gi] + dc
                                v.wait_ge(self.s_pe.sem, pe0 + it + 1)
                                a = acc[:, dc * OWN:(dc + 1) * OWN]
                                self.s_dve.inc(v.tensor_add(a, a, ps[it % 2][:]), 1)

            dven = self.s_dve.n
            with nc.Block() as block:
                @block.scalar
                def _(e):
                    e.wait_ge(self.s_dve.sem, dven)
                    for oc in range(DC):
                        self.s_st.inc(e.dma_start(
                            out=self.out_fm[oc * 128:(oc + 1) * 128, :],
                            in_=acc[:, oc * OWN:(oc + 1) * OWN]), 16)


def _prep_core_inputs(inputs):
    x = np.asarray(inputs["L0_attn_x"], dtype=np.float32)

    def tile_lhs(w, jw):
        wt = np.ascontiguousarray(np.asarray(w, dtype=np.float32).T)
        nfc = D // jw
        return np.ascontiguousarray(
            wt.reshape(DC, 128, nfc, jw).transpose(2, 1, 0, 3))

    shared = {
        "wq_l": tile_lhs(inputs["L0_attn_Wq"], 128),
        "wk_l": tile_lhs(inputs["L0_attn_Wk"], 128),
        "wv_l": tile_lhs(inputs["L0_attn_Wv"], 512),
        "wo_l": tile_lhs(inputs["L0_attn_Wo"], 128),
        "bq": np.ascontiguousarray(np.asarray(inputs["L0_attn_bq"], dtype=np.float32).reshape(1, D)),
        "bk": np.ascontiguousarray(np.asarray(inputs["L0_attn_bk"], dtype=np.float32).reshape(1, D)),
        "bv": np.ascontiguousarray(np.asarray(inputs["L0_attn_bv"], dtype=np.float32).reshape(1, D)),
        "bo": np.ascontiguousarray(np.asarray(inputs["L0_attn_bo"], dtype=np.float32).reshape(1, D)),
        "codes1": np.ascontiguousarray(np.asarray(inputs["L0_ffn_w1_codes"], dtype=np.int32)),
        "codes3": np.ascontiguousarray(np.asarray(inputs["L0_ffn_w3_codes"], dtype=np.int32)),
        "codes2": np.ascontiguousarray(np.asarray(inputs["L0_ffn_w2_codes"], dtype=np.int32)),
        "a1": np.ascontiguousarray(np.asarray(inputs["L0_ffn_w1_absmax"], dtype=np.float32)),
        "a3": np.ascontiguousarray(np.asarray(inputs["L0_ffn_w3_absmax"], dtype=np.float32)),
        "a2": np.ascontiguousarray(np.asarray(inputs["L0_ffn_w2_absmax"], dtype=np.float32)),
    }
    in_maps = []
    for c in range(NCORES):
        seq, half = c // 2, c % 2
        x_t = np.ascontiguousarray(x[seq].T)
        m = dict(shared)
        m["x_t"] = x_t
        m["x_own_t"] = np.ascontiguousarray(x_t[:, half * OWN:(half + 1) * OWN])
        m["q0col"] = np.full((128, 1), float(half * OWN), dtype=np.float32)
        in_maps.append(m)
    return in_maps


def kernel(**inputs):
    in_maps = _prep_core_inputs(inputs)
    nc = MK().build()
    res = run_bass_kernel_spmd(nc, in_maps, core_ids=list(range(NCORES)))
    out = np.empty((B, SEQ, D), dtype=np.float32)
    for c in range(NCORES):
        seq, half = c // 2, c % 2
        out[seq, half * OWN:(half + 1) * OWN, :] = np.asarray(res.results[c]["out_fm"]).T
    return out


if __name__ == "__main__":
    nc = MK().build()
    print("instructions:", nc.n_instructions())
